# version 04
# baseline (speedup 1.0000x reference)
import math
from contextlib import ExitStack

import numpy as np
import ml_dtypes

import concourse.bass as bass
import concourse.mybir as mybir
from concourse.bass_utils import run_bass_kernel_spmd

F32 = mybir.dt.float32
BF16 = mybir.dt.bfloat16
AF = mybir.ActivationFunctionType
ALU = mybir.AluOpType
AX = mybir.AxisListType
BF = ml_dtypes.bfloat16

D = 2048
T = 8192
NCORE = 8
TL = 1024
DCH = 16
NKCH = 64

A_HD = 128
A_SCALE = 1.0 / math.sqrt(128.0)
A_SHIFT = 12.0
A_EPS = 1e-6
ROPE_BASE = 100000.0
B_SCALE = 1.0 / 8.0
B_SHIFT = 10.0
B_EPS = 1e-5
LAMBDA_INIT = 0.8 - 0.6 * math.exp(-0.3)
E_EXP = 8
LN_EPS = 1e-5
NEG_BIG = -1e30

ENGINES = ("tensor", "vector", "scalar", "gpsimd", "sync")

BUILD_STAGES = 5
DEBUG_TAPS = ()
import os
SUBSTAGE = os.environ.get("KM_SUBSTAGE", "full")


class Sem:
    def __init__(self, nc, name):
        self.h = nc.alloc_semaphore(name)
        self.n = 0
        self.name = name


class MK:
    def __init__(self):
        self.nc = bass.Bass("TRN2", target_bir_lowering=False, dynamic_dma_scratch_size=16384)
        self.ops = {e: [] for e in ENGINES}
        self.consts = {}
        self.taps = {}

    def emit(self, eng, fn):
        self.ops[eng].append(fn)

    def wait(self, eng, sem, n=None):
        if n is None:
            n = sem.n
        assert n <= sem.n, f"wait beyond future: {sem.name} {n} > {sem.n}"
        self.emit(eng, lambda E, s=sem.h, n=n: E.wait_ge(s, n))

    @staticmethod
    def _apply(instr, incs):
        for s, k in incs:
            instr.then_inc(s.h, k)

    def _note(self, incs):
        for s, k in incs:
            s.n += k

    def mm(self, out, lhsT, rhs, start=True, stop=True, inc=()):
        incs = tuple(inc)
        def f(E, out=out, lhsT=lhsT, rhs=rhs, start=start, stop=stop, incs=incs):
            self._apply(E.matmul(out, lhsT, rhs, start=start, stop=stop), incs)
        self.emit("tensor", f)
        self._note(incs)

    def tr(self, out, in_, ident, inc=()):
        incs = tuple(inc)
        def f(E, out=out, in_=in_, ident=ident, incs=incs):
            self._apply(E.transpose(out, in_, ident), incs)
        self.emit("tensor", f)
        self._note(incs)

    def act(self, out, in_, func, bias=0.0, scale=1.0, inc=()):
        for sm, k in inc:
            assert sm is self.ss and k == 1
        if isinstance(bias, float):
            self.const(bias)
        if isinstance(scale, float):
            self.const(scale)
        if self.ss.n:
            self.emit("scalar", lambda E, h=self.ss.h, n=self.ss.n: E.wait_ge(h, n))
        incs = ((self.ss, 1),)
        def f(E, out=out, in_=in_, func=func, bias=bias, scale=scale, incs=incs):
            self._apply(E.activation(out=out, in_=in_, func=func, bias=bias, scale=scale), incs)
        self.emit("scalar", f)
        self._note(incs)

    def scopy(self, out, in_, inc=()):
        for sm, k in inc:
            assert sm is self.ss and k == 1
        if self.ss.n:
            self.emit("scalar", lambda E, h=self.ss.h, n=self.ss.n: E.wait_ge(h, n))
        incs = ((self.ss, 1),)
        def f(E, out=out, in_=in_, incs=incs):
            self._apply(E.copy(out=out, in_=in_), incs)
        self.emit("scalar", f)
        self._note(incs)

    def vop(self, name, *args, inc=(), eng="vector", **kw):
        assert eng == "vector"
        for sm, k in inc:
            assert sm is self.sv and k == 1
        if self.sv.n:
            self.emit("vector", lambda E, h=self.sv.h, n=self.sv.n: E.wait_ge(h, n))
        incs = ((self.sv, 1),)
        def f(E, name=name, args=args, kw=kw, incs=incs):
            self._apply(getattr(E, name)(*args, **kw), incs)
        self.emit(eng, f)
        self._note(incs)

    def dma(self, q, out, in_, sem=None, transpose=False):
        incs = ((sem, 16),) if sem is not None else ()
        def f(E, out=out, in_=in_, transpose=transpose, incs=incs):
            self._apply(E.dma_start(out=out, in_=in_, transpose=transpose), incs)
        self.emit(q, f)
        self._note(incs)

    def coll(self, in_ap, out_ap, sem):
        def f(E, in_ap=in_ap, out_ap=out_ap, sem=sem):
            E.collective_compute(
                "AllGather", ALU.bypass, replica_groups=[list(range(NCORE))],
                ins=[in_ap], outs=[out_ap],
            ).then_inc(sem.h, 1)
        self.emit("gpsimd", f)
        sem.n += 1

    def const(self, v):
        v = float(v)
        if v not in self.consts:
            self.consts[v] = len(self.consts)
        return v

    def RESc(self, d, t0=0, t1=TL):
        return self.RES[:, TL * d + t0: TL * d + t1]

    def XOc(self, d, t0=0, t1=TL):
        return self.XO[:, TL * d + t0: TL * d + t1]

    def QBc(self, d, t0=0, t1=TL):
        return self.QB[:, TL * d + t0: TL * d + t1]

    def tap(self, name, ap, shape, dtype=F32):
        if name not in self.want_taps:
            return
        d = self.nc.dram_tensor("tap_" + name, shape, dtype, kind="ExternalOutput").ap()
        self.taps[name] = d
        self.dma("sync", d, ap, sem=self.so)

    def build(self, stages=None, taps=None):
        nc = self.nc
        self.stages = BUILD_STAGES if stages is None else stages
        self.want_taps = set(DEBUG_TAPS if taps is None else taps)

        dri = lambda n, s: nc.dram_tensor(n, s, F32, kind="ExternalInput").ap()
        self.xT = dri("xT", [D, TL])
        self.cos2 = nc.dram_tensor("cos2", [128, TL], BF16, kind="ExternalInput").ap()
        self.sin2 = nc.dram_tensor("sin2", [128, TL], BF16, kind="ExternalInput").ap()
        self.masks_in = nc.dram_tensor("masks", [NCORE, 128, 128], BF16, kind="ExternalInput").ap()
        self.wq = dri("a_wq", [D, 2048])
        self.wk = dri("a_wk", [D, 512])
        self.wv = dri("a_wv", [D, 512])
        self.wo = dri("a_wo", [D, D])
        self.wqkv = dri("b_wqkv", [D, 6144])
        self.wproj = dri("b_wproj", [D, D])
        self.lrows_in = dri("lrows", [1, 256])
        self.gate_w = dri("gate_w", [E_EXP, D])
        self.gup = dri("gup", [E_EXP, 1536, D])
        self.dpw = dri("dpw", [E_EXP, D, 768])
        self.sgw = dri("sgw", [4096, D])
        self.suw = dri("suw", [4096, D])
        self.sdw = dri("sdw", [D, 4096])
        self.segw = dri("segw", [1, D])
        self.yT = nc.dram_tensor("yT", [D, TL], F32, kind="ExternalOutput").ap()

        self.gup_bf = nc.dram_tensor("gup_bf", [E_EXP, 1536, D], BF16)
        self.dp_bf = nc.dram_tensor("dp_bf", [E_EXP, D, 768], BF16)
        self.sg_bf = nc.dram_tensor("sg_bf", [4096, D], BF16)
        self.su_bf = nc.dram_tensor("su_bf", [4096, D], BF16)
        self.sd_bf = nc.dram_tensor("sd_bf", [D, 4096], BF16)
        self.agka_in = nc.dram_tensor("agka_in", [512, TL], BF16)
        self.agka_out = nc.dram_tensor("agka_out", [NCORE * 512, TL], BF16)
        self.agva_in = nc.dram_tensor("agva_in", [TL, 512], BF16)
        self.agva_out = nc.dram_tensor("agva_out", [T, 512], BF16)
        self.agkb_in = nc.dram_tensor("agkb_in", [2048, TL], BF16)
        self.agkb_out = nc.dram_tensor("agkb_out", [NCORE * 2048, TL], BF16)
        self.agvb_in = nc.dram_tensor("agvb_in", [TL, 2048], BF16)
        self.agvb_out = nc.dram_tensor("agvb_out", [T, 2048], BF16)
        self.sig_dram = nc.dram_tensor("sig_dram", [1, TL], F32)

        perm = np.zeros((128, 128), dtype=np.float32)
        for i in range(128):
            perm[i ^ 64, i] = 1.0
        self.ident_bf_d = nc.inline_tensor(np.eye(128, dtype=np.float32).astype(BF), name="ident_bf")
        self.ident_f_d = nc.inline_tensor(np.eye(128, dtype=np.float32), name="ident_f")
        self.perm_d = nc.inline_tensor(perm.astype(BF), name="perm_bf")
        self.onesc_bf_d = nc.inline_tensor(np.ones((128, 1), dtype=BF), name="onesc_bf")
        self.zeros_bf_d = nc.inline_tensor(np.zeros((128, 128), dtype=BF), name="zeros_bf")
        self.onesf_d = nc.inline_tensor(np.ones((128, 128), dtype=np.float32), name="onesf")

        ctx = ExitStack()
        sb = lambda n, s, dt: ctx.enter_context(nc.sbuf_tensor(n, s, dt))
        self.RES = sb("RES", [128, DCH * TL], F32)
        self.XO = sb("XO", [128, DCH * TL], BF16)
        self.QB = sb("QB", [128, DCH * TL], BF16)
        self.WS = sb("WS", [128, 6144], BF16)
        self.KV1 = sb("KV1", [128, T], BF16)
        self.KV2 = sb("KV2", [128, T], BF16)
        self.PTW = sb("PTW", [128, 2048], BF16)
        self.A1 = sb("A1", [128, 1024], BF16)
        self.TMP = sb("TMP", [128, 1024], F32)
        self.COS2 = sb("COS2", [128, TL], BF16)
        self.SIN2 = sb("SIN2", [128, TL], BF16)
        self.MASKS = sb("MASKS", [128, NCORE * 128], BF16)
        self.IDB = sb("IDB", [128, 128], BF16)
        self.IDF = sb("IDF", [128, 128], F32)
        self.PERM = sb("PERM", [128, 128], BF16)
        self.OCB = sb("OCB", [128, 1], BF16)
        self.ONESF = sb("ONESF", [128, 128], F32)
        self.OCF = self.ONESF[:, 0:1]
        self.ROWT = sb("ROWT", [128, 2048], F32)
        self.R1 = self.ROWT[0:1, 0:1024]
        self.R2 = self.ROWT[0:1, 1024:2048]
        self.LROWS = self.ROWT[64:65, 0:256]
        self.LSCR = self.ROWT[64:65, 256:272]
        self.BSCR = sb("BSCR", [1, 8], F32)
        self.LAMN = sb("LAMN", [128, 1], F32)
        self.ROUT = sb("ROUT", [128, 32], F32)
        self.CWTM = sb("CWTM", [128, 64], F32)
        self.GWB = sb("GWB", [128, 144], BF16)
        self.CONSTS = sb("CONSTS", [128, 24], F32)
        self.PS = ctx.enter_context(nc.psum_tensor("PS", [128, 4096], F32))
        self.KV2F = self.KV2[:].bitcast(F32)
        self.A1B = self.A1

        self.st = Sem(nc, "st")
        self.sv = Sem(nc, "sv")
        self.ss = Sem(nc, "ss")
        self.ld = Sem(nc, "ld")
        self.lc = Sem(nc, "lc")
        self.lg = Sem(nc, "lg")
        self.lgp = [Sem(nc, f"lgp{i}") for i in range(4)]
        self.ldd = [Sem(nc, "ldd0"), Sem(nc, "ldd1")]
        self.lws = [Sem(nc, f"lws{i}") for i in range(8)]
        self.lp = Sem(nc, "lp")
        self.cc = Sem(nc, "cc")
        self.so = Sem(nc, "so")

        self.phase_init()
        if self.stages >= 1:
            self.phase_ln("ln1")
        if self.stages >= 2:
            self.phase_attna()
        if self.stages >= 3:
            self.phase_ln("ln2")
            self.phase_ffn("ffn1")
        if self.stages >= 4:
            self.phase_ln("ln3")
            self.phase_attnb()
        if self.stages >= 5:
            self.phase_ln("ln4")
            self.phase_ffn("ffn2")
        self.phase_out()

        cv = sorted(self.consts, key=self.consts.get)
        self.const_np = np.tile(np.array([cv or [0.0]], dtype=np.float32), (128, 1))
        self.const_d = nc.inline_tensor(self.const_np, name="consts_inline")
        for v in cv:
            nc.const_aps.register(F32, v, self.CONSTS[:, self.consts[v]:self.consts[v] + 1])

        with nc.Block() as block:
            for eng in ENGINES:
                def runner(E, eng=eng):
                    for f in self.ops[eng]:
                        f(E)
                getattr(block, eng)(runner)
        ctx.close()
        return nc

    def phase_init(self):
        z, ld = self, self.ld
        def constload(E):
            E.dma_start(out=z.CONSTS[:, 0:self.const_np.shape[1]], in_=self.const_d.ap()).then_inc(ld.h, 16)
        self.emit("sync", constload)
        ld.n += 16
        z.dma("sync", z.IDB[:], z.ident_bf_d.ap(), sem=ld)
        z.dma("sync", z.IDF[:], z.ident_f_d.ap(), sem=ld)
        z.dma("sync", z.PERM[:], z.perm_d.ap(), sem=ld)
        z.dma("sync", z.OCB[:], z.onesc_bf_d.ap(), sem=ld)
        z.dma("sync", z.ONESF[:], z.onesf_d.ap(), sem=ld)
        z.dma("sync", z.COS2[:], z.cos2, sem=ld)
        z.dma("sync", z.SIN2[:], z.sin2, sem=ld)
        z.dma("sync", z.MASKS[:].rearrange("p (r n) -> p r n", r=NCORE), z.masks_in.rearrange("r p n -> p r n"), sem=ld)
        z.dma("sync", z.LROWS, z.lrows_in, sem=ld)
        for d in range(DCH):
            z.dma("sync", z.RESc(d), z.xT[128 * d:128 * (d + 1), :], sem=ld)
        init_ld = ld.n

        if self.stages >= 3:
            for e in range(E_EXP):
                z.dma("gpsimd", z.gup_bf.ap()[e], z.gup[e], sem=z.lp)
                z.dma("gpsimd", z.dp_bf.ap()[e], z.dpw[e], sem=z.lp)
            z.dma("gpsimd", z.sg_bf.ap(), z.sgw, sem=z.lp)
            z.dma("gpsimd", z.su_bf.ap(), z.suw, sem=z.lp)
            z.dma("gpsimd", z.sd_bf.ap(), z.sdw, sem=z.lp)
        self.prepass_lp = z.lp.n

        for eng in ENGINES:
            z.wait(eng, ld, init_ld)

        L, C = z.LROWS, z.LSCR
        z.vop("tensor_tensor", C[:, 8:9].rearrange("p n -> p n") if False else z.ROWT[64:65, 272:336], L[:, 0:64], L[:, 64:128], ALU.mult)
        z.vop("tensor_tensor", z.ROWT[64:65, 336:400], L[:, 128:192], L[:, 192:256], ALU.mult)
        z.vop("reduce_sum", C[:, 0:1], z.ROWT[64:65, 272:336], axis=AX.X)
        z.vop("reduce_sum", C[:, 1:2], z.ROWT[64:65, 336:400], inc=[(z.sv, 1)], axis=AX.X)
        z.wait("scalar", z.sv)
        z.act(C[:, 2:3], C[:, 0:1], AF.Exp)
        z.act(C[:, 3:4], C[:, 1:2], AF.Exp, inc=[(z.ss, 1)])
        z.wait("vector", z.ss)
        z.vop("tensor_tensor", C[:, 4:5], C[:, 3:4], C[:, 2:3], ALU.subtract)
        z.vop("tensor_scalar_sub", C[:, 5:6], C[:, 4:5], float(LAMBDA_INIT), inc=[(z.sv, 1)])
        z.wait("tensor", z.sv)
        z.mm(z.PS[:, 0:1], z.ONESF[64:65, 0:128], C[:, 5:6], inc=[(z.st, 1)])
        z.wait("vector", z.st)
        z.vop("tensor_copy", z.LAMN[:], z.PS[:, 0:1], inc=[(z.sv, 1)])
        self.lamn_ready = z.sv.n

    def barrier_tv(self):
        z = self
        z.vop("memset", z.BSCR[0:1, 0:1], 0.0)
        if self.ss.n:
            self.emit("scalar", lambda E, h=self.ss.h, n=self.ss.n: E.wait_ge(h, n))
        def smz(E):
            self._apply(E.memzero(z.BSCR[0:1, 1:2]), ((z.ss, 1),))
        self.emit("scalar", smz)
        z.ss.n += 1
        z.wait("tensor", z.sv)
        z.wait("tensor", z.ss)
        def tmark(E):
            E.nop().then_inc(z.st.h, 1)
        z.emit("tensor", tmark)
        z.st.n += 1
        for eng in ("vector", "scalar", "sync", "gpsimd"):
            z.wait(eng, z.st)

    def phase_ln(self, tag):
        z = self
        self.barrier_tv()
        SQ = [z.KV2F[:, 0:512], z.KV2F[:, 512:1024]]
        for th in range(2):
            t0, t1 = 512 * th, 512 * (th + 1)
            SUM = z.PS[0:1, 2048:2560]
            SSQ = z.PS[0:1, 3072:3584]
            sqmm = [0] * DCH
            for d in range(DCH):
                z.mm(SUM, z.OCF[:], z.RESc(d, t0, t1), start=(d == 0), stop=(d == DCH - 1),
                     inc=([(z.st, 1)] if d == DCH - 1 else ()))
                if d >= 2:
                    z.wait("scalar", z.st, sqmm[d - 2])
                z.act(SQ[d % 2], z.RESc(d, t0, t1), AF.Square, inc=[(z.ss, 1)])
                z.wait("tensor", z.ss, z.ss.n)
                z.mm(SSQ, z.OCF[:], SQ[d % 2], start=(d == 0), stop=(d == DCH - 1),
                     inc=[(z.st, 1)])
                sqmm[d] = z.st.n
            z.wait("vector", z.st, z.st.n)
            m = z.R1[:, 0:512]
            q = z.R1[:, 512:1024]
            t = z.R2[:, 512:1024]
            v = z.R2[:, 0:512]
            z.vop("tensor_scalar_mul", m, SUM, 1.0 / D)
            z.vop("tensor_scalar_mul", q, SSQ, 1.0 / D)
            z.vop("tensor_tensor", t, m, m, ALU.mult)
            z.vop("tensor_tensor", v, q, t, ALU.subtract, inc=[(z.sv, 1)])
            z.wait("scalar", z.sv)
            z.act(v, v, AF.Sqrt, bias=float(LN_EPS), inc=[(z.ss, 1)])
            z.wait("vector", z.ss)
            z.vop("reciprocal", t, v)
            z.vop("tensor_tensor", m, m, t, ALU.mult)
            z.vop("tensor_scalar_mul", m, m, -1.0, inc=[(z.sv, 1)])
            z.wait("tensor", z.sv)
            BCA = z.PS[:, 0:512]
            BCB = z.PS[:, 1024:1536]
            z.mm(BCA, z.ONESF[0:1, 0:128], t)
            z.mm(BCB, z.ONESF[0:1, 0:128], m, inc=[(z.st, 1)])
            z.wait("vector", z.st)
            for d in range(DCH):
                z.vop("tensor_tensor", z.TMP[:, 0:512], z.RESc(d, t0, t1), BCA, ALU.mult)
                z.vop("tensor_tensor", z.XOc(d, t0, t1), z.TMP[:, 0:512], BCB, ALU.add,
                      inc=([(z.sv, 1)] if d == DCH - 1 else ()))
            z.wait("tensor", z.sv)
        self.barrier_tv()
        self.tap(tag, z.XO[:], [128, DCH * TL], BF16)

    def proj_stream(self, w_dram, noc, consumer, rhs=None):
        z = self
        if rhs is None:
            rhs = z.XOc
        regions = [[z.PS[:, 0:1024], z.PS[:, 1024:2048]],
                   [z.PS[:, 2048:3072], z.PS[:, 3072:4096]]]
        blocks = [list(range(b, min(b + 2, noc))) for b in range(0, noc, 2)]
        z.wait("tensor", z.sv, z.sv.n)
        z.wait("tensor", z.ss, z.ss.n)
        lastmarks = [z.st.n] * 4
        ps_free = {}
        prev = None
        for bi, blk in enumerate(blocks):
            regs = regions[bi % 2]
            if bi >= 2:
                z.wait("tensor", z.sv, ps_free[bi - 2])
            w0 = 128 * blk[0]
            wcols = 128 * len(blk)
            for di in range(DCH):
                slot = z.PTW[:, 512 * (di % 4): 512 * (di % 4) + wcols]
                z.wait("gpsimd", z.st, lastmarks[di % 4])
                z.dma("gpsimd", slot, w_dram[128 * di:128 * (di + 1), w0:w0 + wcols], sem=z.lgp[di % 4])
                z.wait("tensor", z.lgp[di % 4], z.lgp[di % 4].n)
                for oi, oc in enumerate(blk):
                    z.mm(regs[oi][:, 0:512], slot[:, 128 * oi:128 * (oi + 1)],
                         rhs(di, 0, 512), start=(di == 0), stop=(di == DCH - 1))
                    z.mm(regs[oi][:, 512:1024], slot[:, 128 * oi:128 * (oi + 1)],
                         rhs(di, 512, 1024), start=(di == 0), stop=(di == DCH - 1),
                         inc=([(z.st, 1)] if oi == len(blk) - 1 else ()))
                lastmarks[di % 4] = z.st.n
            mmdone = z.st.n
            if prev is not None:
                pbi, pblk, pregs, pmm = prev
                for oi, oc in enumerate(pblk):
                    consumer(oc, pregs[oi], pmm)
                ps_free[pbi] = z.sv.n
            prev = (bi, blk, regs, mmdone)
        pbi, pblk, pregs, pmm = prev
        for oi, oc in enumerate(pblk):
            consumer(oc, pregs[oi], pmm)
        z.wait("tensor", z.sv, z.sv.n)
        z.wait("tensor", z.ss, z.ss.n)

    def phase_attna(self):
        z = self
        self.barrier_tv()
        self.vstage = {"count": 0, "nvch": 4}

        self.proj_stream(z.wq, 16, lambda oc, ps, mk: self.rope_rms_evac(oc, ps, mk, "q", oc))
        self.proj_stream(z.wk, 4, lambda oc, ps, mk: self.rope_rms_evac(oc, ps, mk, "k", oc))
        self.proj_stream(z.wv, 4, lambda oc, ps, mk: self.v_evac(oc, ps, mk, 4, z.agva_in))

        if SUBSTAGE == "qkv":
            self.barrier_tv()
            return
        z.wait("gpsimd", z.ld, z.ld.n)
        z.wait("gpsimd", z.lc, z.lc.n)
        z.wait("gpsimd", z.st, z.st.n)
        z.coll(z.agka_in.ap(), z.agka_out.ap(), z.cc)
        z.coll(z.agva_in.ap(), z.agva_out.ap(), z.cc)
        ccn = z.cc.n
        z.wait("sync", z.cc, ccn)
        z.wait("scalar", z.cc, ccn)

        self.barrier_tv()
        ngrp = {"ag": 0, "g1": 1}.get(SUBSTAGE, 4)
        if SUBSTAGE == "ag":
            tq = self.nc.dram_tensor("tap_qb", [128, DCH * TL], BF16, kind="ExternalOutput").ap()
            tk = self.nc.dram_tensor("tap_agk", [4096, TL], BF16, kind="ExternalOutput").ap()
            tv = self.nc.dram_tensor("tap_agv", [T, 512], BF16, kind="ExternalOutput").ap()
            z.dma("sync", tq, z.QB[:], sem=z.so)
            z.dma("gpsimd", tk, z.agka_out.ap(), sem=z.so)
            z.dma("gpsimd", tv, z.agva_out.ap(), sem=z.so)
            z.wait("sync", z.so, z.so.n)
            return
        for g in range(ngrp):
            if g > 0:
                z.wait("sync", z.st, z.st.n)
                z.wait("scalar", z.st, z.st.n)
            for r in range(NCORE):
                z.dma("sync", z.KV1[:, TL * r:TL * (r + 1)],
                      z.agka_out.ap()[512 * r + 128 * g: 512 * r + 128 * (g + 1), :], sem=z.ld)
            vsrc = z.agva_out.ap().rearrange("(c p) n -> p c n", p=128)
            for qt in range(4):
                z.dma("scalar",
                      z.KV2[:].rearrange("p (c n) -> p c n", c=64)[:, 16 * qt:16 * (qt + 1), :],
                      vsrc[:, 16 * qt:16 * (qt + 1), 128 * g:128 * (g + 1)], sem=z.lc)
            z.wait("tensor", z.ld, z.ld.n)
            z.wait("tensor", z.lc, z.lc.n)
            for hh in range(4):
                h = 4 * g + hh
                self.attn_core(z.QB, TL * h, 0, 128, A_SCALE, A_SHIFT)
                self.attn_epilogue_a(h)

        self.barrier_tv()
        if SUBSTAGE == "att":
            txo = self.nc.dram_tensor("tap_xo", [128, DCH * TL], BF16, kind="ExternalOutput").ap()
            z.dma("sync", txo, z.XO[:], sem=z.so)
            z.wait("sync", z.so, z.so.n)
            return
        if SUBSTAGE == "full":
            def c_out(oc, ps, mk):
                z.wait("vector", z.st, mk)
                z.vop("tensor_tensor", z.RESc(oc), ps[:, 0:1024], z.RESc(oc), ALU.add, inc=[(z.sv, 1)])
            self.proj_stream(z.wo, DCH, c_out)
            self.barrier_tv()
        self.tap("attna", z.RES[:], [128, DCH * TL], F32)

    def rope_rms_evac(self, oc, ps, mmdone, dst, head):
        z = self
        rtap = os.environ.get("KM_RTAP") and dst == "q" and head == 0
        xb = z.KV2[:, 0:1024]
        swb = z.KV2[:, 1024:2048]
        SQ = z.KV2F[:, 1024:2048]
        z.wait("scalar", z.st, z.st.n)
        z.wait("scalar", z.sv, z.sv.n)
        z.scopy(xb, ps[:, 0:1024], inc=[(z.ss, 1)])
        z.wait("tensor", z.ss, z.ss.n)
        z.mm(ps[:, 0:512], z.PERM[:], xb[:, 0:512])
        z.mm(ps[:, 512:1024], z.PERM[:], xb[:, 512:1024], inc=[(z.st, 1)])
        z.wait("scalar", z.st, z.st.n)
        z.scopy(swb, ps[:, 0:1024], inc=[(z.ss, 1)])
        z.wait("vector", z.ss, z.ss.n)
        z.vop("tensor_tensor", z.TMP[:], xb, z.COS2[:], ALU.mult)
        z.vop("tensor_tensor", z.A1[:], swb, z.SIN2[:], ALU.mult)
        z.vop("tensor_tensor", z.A1[:], z.TMP[:], z.A1[:], ALU.add, inc=[(z.sv, 1)])
        rope_done = z.sv.n
        if rtap:
            t1 = self.nc.dram_tensor("rtap_a1", [128, 1024], BF16, kind="ExternalOutput").ap()
            z.wait("sync", z.sv, rope_done)
            z.dma("sync", t1, z.A1[:], sem=z.so)
            z.wait("vector", z.so, z.so.n)
            z.wait("scalar", z.so, z.so.n)
            z.wait("tensor", z.so, z.so.n)
        z.wait("scalar", z.sv, rope_done)
        z.act(SQ, z.A1[:], AF.Square, inc=[(z.ss, 1)])
        z.wait("tensor", z.ss, z.ss.n)
        z.wait("tensor", z.sv, rope_done)
        SSQ = ps[0:1, 0:1024]
        z.mm(SSQ[:, 0:512], z.OCF[:], SQ[:, 0:512])
        z.mm(SSQ[:, 512:1024], z.OCF[:], SQ[:, 512:1024], inc=[(z.st, 1)])
        z.wait("scalar", z.st, z.st.n)
        z.act(z.R1, SSQ, AF.Sqrt, scale=1.0 / A_HD, bias=float(A_EPS), inc=[(z.ss, 1)])
        z.wait("vector", z.ss, z.ss.n)
        z.vop("reciprocal", z.R2, z.R1, inc=[(z.sv, 1)])
        z.wait("tensor", z.sv, z.sv.n)
        z.mm(ps[:, 0:512], z.ONESF[0:1, 0:128], z.R2[:, 0:512])
        z.mm(ps[:, 512:1024], z.ONESF[0:1, 0:128], z.R2[:, 512:1024], inc=[(z.st, 1)])
        z.wait("vector", z.st, z.st.n)
        if dst == "q":
            z.vop("tensor_tensor", z.QBc(head), z.A1[:], ps[:, 0:1024], ALU.mult, inc=[(z.sv, 1)])
        else:
            stg = z.WS[:, 1024 * head:1024 * (head + 1)]
            z.vop("tensor_tensor", stg, z.A1[:], ps[:, 0:1024], ALU.mult, inc=[(z.sv, 1)])
            z.wait("sync", z.sv, z.sv.n)
            z.dma("sync", self.agka_in.ap()[128 * head:128 * (head + 1), :], stg, sem=z.ld)

    def v_evac(self, vc, ps, mmdone, nvch, agdram):
        z = self
        vb = z.A1B[:, 0:1024]
        z.wait("scalar", z.st, z.st.n)
        z.wait("scalar", z.sv, z.sv.n)
        z.scopy(vb, ps[:, 0:1024], inc=[(z.ss, 1)])
        z.wait("tensor", z.ss, z.ss.n)
        psb = ps.bitcast(BF16)
        cpmark = [z.sv.n] * 8
        for tc in range(8):
            z.wait("tensor", z.sv, cpmark[tc - 2] if tc >= 2 else cpmark[0])
            z.tr(psb[:, 1024 * (tc % 2):1024 * (tc % 2) + 128], vb[:, 128 * tc:128 * (tc + 1)],
                 z.IDB[:], inc=[(z.st, 1)])
            z.wait("vector", z.st, z.st.n)
            if nvch == 4:
                stg = z.KV1[:, 512 * tc + 128 * vc: 512 * tc + 128 * (vc + 1)]
            else:
                col = 2048 * tc + 128 * vc
                stg = z.KV1[:, col:col + 128] if tc < 4 else z.KV2[:, col - 8192:col - 8192 + 128]
            z.vop("tensor_copy", stg, psb[:, 1024 * (tc % 2):1024 * (tc % 2) + 128],
                  inc=[(z.sv, 1)])
            cpmark[tc] = z.sv.n
        self.vstage["count"] += 1
        if self.vstage["count"] == nvch:
            self.vstage["count"] = 0
            z.wait("scalar", z.sv, z.sv.n)
            for tc in range(8):
                if nvch == 4:
                    src = z.KV1[:, 512 * tc:512 * tc + 512]
                else:
                    src = z.KV1[:, 2048 * tc:2048 * tc + 2048] if tc < 4 else \
                        z.KV2[:, 2048 * (tc - 4):2048 * (tc - 4) + 2048]
                z.dma("scalar", agdram.ap()[128 * tc:128 * (tc + 1), :], src, sem=z.lc)

    def attn_core(self, qbuf, qcol, qrow0, kw, scale, shift):
        z = self
        PV = z.PS[:, 2048:3072]
        Zr = z.PS[0:1, 3072:4096]
        exp_done = [0] * NKCH
        pv_done = [0] * NKCH
        for c in range(NKCH):
            r, jk = c // 8, c % 8
            q0 = 128 * jk
            Sreg = z.PS[:, 0:1024] if c % 2 == 0 else z.PS[:, 1024:2048]
            Preg = z.PTW[:, 0:1024] if c % 2 == 0 else z.PTW[:, 1024:2048]
            if c >= 2:
                z.wait("tensor", z.ss, exp_done[c - 2])
            spans = [(q0, 512), (512, 1024)] if q0 < 512 else [(q0, 1024)]
            for si, (a, b) in enumerate(spans):
                z.mm(Sreg[:, a:b], z.KV1[qrow0:qrow0 + kw, 128 * c:128 * (c + 1)],
                     qbuf[qrow0:qrow0 + kw, qcol + a:qcol + b],
                     start=True, stop=True,
                     inc=([(z.st, 1)] if si == len(spans) - 1 else ()))
            smm = z.st.n
            z.wait("scalar", z.st, smm)
            if c >= 2:
                z.wait("scalar", z.st, pv_done[c - 2])
            z.act(Preg[:, q0:1024], Sreg[:, q0:1024], AF.Exp, bias=-float(shift),
                  scale=float(scale), inc=[(z.ss, 1)])
            exp_done[c] = z.ss.n
            z.wait("vector", z.ss, exp_done[c])
            z.vop("tensor_tensor", Preg[:, q0:q0 + 128], Preg[:, q0:q0 + 128],
                  z.MASKS[:, 128 * r:128 * (r + 1)], ALU.mult, inc=[(z.sv, 1)])
            z.wait("tensor", z.sv, z.sv.n)
            for (a, b) in spans:
                stp = (c == 59) if b <= 512 else (c == NKCH - 1)
                z.mm(Zr[:, a:b], z.OCB[:], Preg[:, a:b], start=(c == 0), stop=stp)
            for si, (a, b) in enumerate(spans):
                stp = (c == 59) if b <= 512 else (c == NKCH - 1)
                z.mm(PV[:, a:b], z.KV2[:, 128 * c:128 * (c + 1)], Preg[:, a:b],
                     start=(c == 0), stop=stp,
                     inc=([(z.st, 1)] if si == len(spans) - 1 else ()))
            pv_done[c] = z.st.n
        self._attn_last = (PV, Zr, z.st.n)

    def _inv_z_bcast(self):
        z = self
        PV, Zr, pvn = self._attn_last
        z.wait("vector", z.st, pvn)
        z.vop("reciprocal", z.R1, Zr, inc=[(z.sv, 1)])
        z.wait("tensor", z.sv, z.sv.n)
        B = z.PS[:, 0:1024]
        z.mm(B[:, 0:512], z.ONESF[0:1, 0:128], z.R1[:, 0:512])
        z.mm(B[:, 512:1024], z.ONESF[0:1, 0:128], z.R1[:, 512:1024], inc=[(z.st, 1)])
        z.wait("scalar", z.st, z.st.n)
        z.scopy(z.TMP[:], B, inc=[(z.ss, 1)])
        return PV

    def attn_epilogue_a(self, h):
        z = self
        PV = self._inv_z_bcast()
        z.wait("vector", z.ss, z.ss.n)
        z.vop("tensor_tensor", z.XOc(h), PV, z.TMP[:], ALU.mult, inc=[(z.sv, 1)])
        z.wait("tensor", z.ss, z.ss.n)

    def phase_attnb(self):
        z = self
        self.barrier_tv()
        self.vstage = {"count": 0, "nvch": 16}

        def c_b(oc, ps, mk):
            if oc < 8:
                self.plain_evac_q(ps, mk, z.QBc(oc))
            elif oc < 16:
                self.plain_evac_k(oc - 8, ps, mk, 0)
            elif oc < 24:
                self.plain_evac_q(ps, mk, z.QBc(8 + (oc - 16)))
            elif oc < 32:
                self.plain_evac_k(oc - 24, ps, mk, 1024)
            else:
                self.v_evac(oc - 32, ps, mk, 16, z.agvb_in)

        self.proj_stream(z.wqkv, 48, c_b)

        z.wait("gpsimd", z.ld, z.ld.n)
        z.wait("gpsimd", z.lc, z.lc.n)
        z.wait("gpsimd", z.st, z.st.n)
        z.coll(z.agkb_in.ap(), z.agkb_out.ap(), z.cc)
        z.coll(z.agvb_in.ap(), z.agvb_out.ap(), z.cc)
        ccn = z.cc.n
        z.wait("sync", z.cc, ccn)
        z.wait("scalar", z.cc, ccn)

        self.barrier_tv()
        for h in range(16):
            hp = 64 * (h % 2)
            qh = h // 2
            z.wait("sync", z.st, z.st.n)
            z.wait("scalar", z.st, z.st.n)
            for r in range(NCORE):
                z.dma("sync", z.KV1[hp:hp + 64, TL * r:TL * (r + 1)],
                      z.agkb_out.ap()[2048 * r + 64 * h: 2048 * r + 64 * (h + 1), :], sem=z.ld)
            vsrc = z.agvb_out.ap().rearrange("(c p) n -> p c n", p=128)
            for qt in range(4):
                z.dma("scalar",
                      z.KV2[:].rearrange("p (c n) -> p c n", c=64)[:, 16 * qt:16 * (qt + 1), :],
                      vsrc[:, 16 * qt:16 * (qt + 1), 128 * h:128 * (h + 1)], sem=z.lc)
            z.wait("tensor", z.ld, z.ld.n)
            z.wait("tensor", z.lc, z.lc.n)
            self.attn_core(z.QB, TL * qh, hp, 64, B_SCALE, B_SHIFT)
            self.attn_epilogue_b1()
            z.wait("sync", z.st, z.st.n)
            for r in range(NCORE):
                z.dma("sync", z.KV1[hp:hp + 64, TL * r:TL * (r + 1)],
                      z.agkb_out.ap()[2048 * r + 1024 + 64 * h: 2048 * r + 1024 + 64 * (h + 1), :],
                      sem=z.ld)
            z.wait("tensor", z.ld, z.ld.n)
            self.attn_core(z.QB, TL * (8 + qh), hp, 64, B_SCALE, B_SHIFT)
            self.attn_epilogue_b2(h)

        self.barrier_tv()
        def c_out(oc, ps, mk):
            z.wait("vector", z.st, mk)
            z.vop("tensor_tensor", z.RESc(oc), ps[:, 0:1024], z.RESc(oc), ALU.add, inc=[(z.sv, 1)])
        self.proj_stream(z.wproj, DCH, c_out)
        self.barrier_tv()
        self.tap("attnb", z.RES[:], [128, DCH * TL], F32)

    def plain_evac_q(self, ps, mk, dst):
        z = self
        z.wait("vector", z.st, mk)
        z.vop("tensor_copy", dst, ps[:, 0:1024], inc=[(z.sv, 1)])

    def plain_evac_k(self, kc, ps, mk, rowoff):
        z = self
        z.wait("vector", z.st, mk)
        z.wait("vector", z.ld, z.ld.n)
        stg = z.WS[:, 1024 * (kc % 4):1024 * (kc % 4) + 1024]
        z.vop("tensor_copy", stg, ps[:, 0:1024], inc=[(z.sv, 1)])
        z.wait("sync", z.sv, z.sv.n)
        z.dma("sync", self.agkb_in.ap()[rowoff + 128 * kc: rowoff + 128 * (kc + 1), :], stg, sem=z.ld)

    def attn_epilogue_b1(self):
        z = self
        PV = self._inv_z_bcast()
        z.wait("vector", z.ss, z.ss.n)
        z.vop("tensor_tensor", z.A1[:], PV, z.TMP[:], ALU.mult, inc=[(z.sv, 1)])
        z.wait("tensor", z.ss, z.ss.n)

    def attn_epilogue_b2(self, h):
        z = self
        PV = self._inv_z_bcast()
        z.wait("vector", z.ss, z.ss.n)
        z.vop("tensor_tensor", z.TMP[:], PV, z.TMP[:], ALU.mult)
        z.wait("vector", z.sv, self.lamn_ready)
        z.vop("scalar_tensor_tensor", z.TMP[:], z.TMP[:], z.LAMN[:, 0:1], z.A1[:],
              ALU.mult, ALU.add, inc=[(z.sv, 1)])
        diff_done = z.sv.n
        z.wait("scalar", z.sv, diff_done)
        KV1F = z.KV1[:].bitcast(F32)
        z.act(KV1F[:, 0:1024], z.TMP[:], AF.Square, inc=[(z.ss, 1)])
        z.wait("tensor", z.ss, z.ss.n)
        z.wait("tensor", z.sv, diff_done)
        SSQ = z.PS[0:1, 3072:4096]
        z.mm(SSQ[:, 0:512], z.OCF[:], KV1F[:, 0:512])
        z.mm(SSQ[:, 512:1024], z.OCF[:], KV1F[:, 512:1024], inc=[(z.st, 1)])
        z.wait("scalar", z.st, z.st.n)
        z.act(z.R2, SSQ, AF.Sqrt, scale=1.0 / 128.0, bias=float(B_EPS), inc=[(z.ss, 1)])
        z.wait("vector", z.ss, z.ss.n)
        z.vop("reciprocal", z.R1, z.R2)
        z.vop("tensor_scalar_mul", z.R1, z.R1, float(1.0 - LAMBDA_INIT),
              inc=[(z.sv, 1)])
        z.wait("tensor", z.sv, z.sv.n)
        B = z.PS[:, 0:1024]
        z.mm(B[:, 0:512], z.ONESF[0:1, 0:128], z.R1[:, 0:512])
        z.mm(B[:, 512:1024], z.ONESF[0:1, 0:128], z.R1[:, 512:1024], inc=[(z.st, 1)])
        z.wait("vector", z.st, z.st.n)
        z.vop("tensor_tensor", z.XOc(h), B, z.TMP[:], ALU.mult, inc=[(z.sv, 1)])
        z.wait("tensor", z.sv, z.sv.n)

    def phase_ffn(self, tag):
        z = self
        self.barrier_tv()
        if tag == "ffn1":
            self.ffn_prep()
            z.wait("sync", z.lp, self.prepass_lp)
            z.wait("scalar", z.lp, self.prepass_lp)
            z.wait("gpsimd", z.lp, self.prepass_lp)
        self.ffn_router()
        self._ws_guard = z.st.n
        for hf in range(2):
            for eg in range(2):
                for ei in range(4):
                    self.ffn_expert_gu(4 * eg + ei, ei, hf)
                if hf == 0 and eg == 0 and tag == "ffn1" and os.environ.get("KM_FTAP"):
                    tq = self.nc.dram_tensor("tap_act", [128, 12288], BF16, kind="ExternalOutput").ap()
                    z.wait("sync", z.sv, z.sv.n)
                    z.dma("sync", tq, z.QB[:, 0:12288], sem=z.so)
                    z.wait("tensor", z.so, z.so.n)
                    z.wait("vector", z.so, z.so.n)
                    z.wait("scalar", z.so, z.so.n)
                self.ffn_expert_down(eg, hf)
            self.ffn_shared(hf)
        self.barrier_tv()
        self.tap(tag, z.RES[:], [128, DCH * TL], F32)

    def ffn_prep(self):
        z = self
        z.dma("sync", z.TMP[0:8, 0:1024], z.gate_w[:, 0:1024], sem=z.ld)
        z.dma("sync", z.ROWT[32:40, 0:1024], z.gate_w[:, 1024:2048], sem=z.ld)
        z.dma("sync", z.R1, z.segw[0:1, 0:1024], sem=z.ld)
        z.dma("sync", z.R2, z.segw[0:1, 1024:2048], sem=z.ld)
        z.wait("tensor", z.ld, z.ld.n)
        self.GWT = z.GWB[:, 0:128]
        self.SEGT = z.GWB[:, 128:144]
        psf = z.PS[:, 0:512]
        for d in range(DCH):
            src = z.TMP[0:8, 128 * d:128 * (d + 1)] if d < 8 else z.ROWT[32:40, 128 * (d - 8):128 * (d - 7)]
            ident = z.IDF[0:8, 0:8] if d < 8 else z.IDF[32:40, 32:40]
            z.tr(psf[:, 0:8], src, ident, inc=[(z.st, 1)])
            z.wait("vector", z.st, z.st.n)
            z.vop("tensor_copy", self.GWT[:, 8 * d:8 * (d + 1)], psf[:, 0:8], inc=[(z.sv, 1)])
            z.wait("tensor", z.sv, z.sv.n)
            srs = z.R1[:, 128 * d:128 * (d + 1)] if d < 8 else z.R2[:, 128 * (d - 8):128 * (d - 7)]
            z.tr(z.PS[:, 512:513], srs, z.IDF[0:1, 0:1], inc=[(z.st, 1)])
            z.wait("vector", z.st, z.st.n)
            z.vop("tensor_copy", self.SEGT[:, d:d + 1], z.PS[:, 512:513], inc=[(z.sv, 1)])
            z.wait("tensor", z.sv, z.sv.n)

    def ffn_router(self):
        z = self
        LG = z.PS[0:8, 3072:4096]
        SEG = z.PS[0:1, 2048:3072]
        for d in range(DCH):
            last = d == DCH - 1
            z.mm(LG[:, 0:512], self.GWT[:, 8 * d:8 * (d + 1)], z.XOc(d, 0, 512), start=(d == 0), stop=last)
            z.mm(LG[:, 512:1024], self.GWT[:, 8 * d:8 * (d + 1)], z.XOc(d, 512, 1024), start=(d == 0), stop=last)
            z.mm(SEG[:, 0:512], self.SEGT[:, d:d + 1], z.XOc(d, 0, 512), start=(d == 0), stop=last)
            z.mm(SEG[:, 512:1024], self.SEGT[:, d:d + 1], z.XOc(d, 512, 1024), start=(d == 0), stop=last,
                 inc=([(z.st, 1)] if last else ()))
        segmm = z.st.n
        z.wait("scalar", z.st, segmm)
        z.act(z.R1, SEG, AF.Sigmoid, inc=[(z.ss, 1)])
        z.wait("sync", z.ss, z.ss.n)
        z.dma("sync", self.sig_dram.ap(), z.R1, sem=z.ld)
        self.sig_ready = z.ld.n
        z.wait("vector", z.ld, z.ld.n)
        z.wait("vector", z.st, segmm)
        z.vop("tensor_copy", z.TMP[0:8, :], LG, inc=[(z.sv, 1)])
        z.wait("tensor", z.sv, z.sv.n)
        LTM = z.ROUT[:, 0:8]
        CW = z.ROUT[:, 8:16]
        S1 = z.ROUT[:, 16:17]
        S2 = z.ROUT[:, 17:18]
        S3 = z.ROUT[:, 18:19]
        T1 = z.ROUT[:, 24:32]
        psf = z.PS[:, 0:512]
        for tc in range(8):
            z.tr(psf[:, 0:8], z.TMP[0:8, 128 * tc:128 * (tc + 1)], z.IDF[0:8, 0:8], inc=[(z.st, 1)])
            z.wait("vector", z.st, z.st.n)
            z.vop("tensor_copy", LTM, psf[:, 0:8], inc=[(z.sv, 1)])
            z.vop("reduce_max", S1, LTM, axis=AX.X)
            z.vop("tensor_scalar_sub", T1, LTM, S1[:, 0:1])
            z.vop("tensor_single_scalar", CW, T1, 0.0, ALU.is_ge)
            z.vop("scalar_tensor_tensor", CW, CW, float(NEG_BIG), LTM, ALU.mult, ALU.add)
            z.vop("reduce_max", S2, CW, inc=[(z.sv, 1)], axis=AX.X)
            z.wait("scalar", z.sv, z.sv.n)
            z.act(T1, T1, AF.Exp, inc=[(z.ss, 1)])
            z.wait("vector", z.ss, z.ss.n)
            z.vop("tensor_single_scalar", CW, LTM, S2[:, 0:1], ALU.is_ge)
            z.vop("tensor_tensor", CW, T1, CW, ALU.mult)
            z.vop("reduce_sum", S3, CW, axis=AX.X)
            z.vop("reciprocal", S3, S3)
            z.vop("tensor_scalar_mul", z.CWTM[:, 8 * tc:8 * (tc + 1)], CW, S3[:, 0:1],
                  inc=[(z.sv, 1)])
            z.wait("tensor", z.sv, z.sv.n)
        self.cw_ready = z.sv.n

    def cw_bcast(self, e, hf, dstps):
        z = self
        psf = z.PS[0:1, 3072:3584]
        for i in range(4):
            tc = 4 * hf + i
            z.tr(psf[:, 128 * i:128 * (i + 1)], z.CWTM[:, 8 * tc + e:8 * tc + e + 1], z.IDF[:],
                 inc=([(z.st, 1)] if i == 3 else ()))
        z.wait("vector", z.st, z.st.n)
        z.vop("tensor_copy", z.R1[:, 0:512], psf, inc=[(z.sv, 1)])
        z.wait("tensor", z.sv, z.sv.n)
        z.mm(dstps, z.ONESF[0:1, 0:128], z.R1[:, 0:512], inc=[(z.st, 1)])
        return z.st.n

    def ffn_expert_gu(self, e, ei, hf):
        z = self
        t0 = 512 * hf
        gps = [z.PS[:, 512 * f:512 * (f + 1)] for f in range(3)]
        ups = [z.PS[:, 1536 + 512 * f:1536 + 512 * (f + 1)] for f in range(3)]
        cwb = z.PS[:, 3072:3584]
        z.wait("tensor", z.sv, z.sv.n)
        cwb_ready = self.cw_bcast(e, hf, cwb)
        for ch in range(2):
            z.wait("tensor", z.sv, z.sv.n)
            z.wait("tensor", z.ss, z.ss.n)
            ddmark = list(getattr(self, "_gu_ddmark", [z.st.n] * 8))
            guard = getattr(self, "_ws_guard", 0)
            ddmark = [max(m, guard) for m in ddmark]
            for d in range(DCH):
                sl = d % 8
                z.wait("sync", z.st, ddmark[sl])
                z.wait("scalar", z.st, ddmark[sl])
                z.dma("sync", z.WS[:, 768 * sl:768 * sl + 384],
                      self.gup_bf.ap()[e][384 * ch:384 * (ch + 1), 128 * d:128 * (d + 1)],
                      sem=z.lws[sl], transpose=True)
                z.dma("sync", z.WS[:, 768 * sl + 384:768 * sl + 768],
                      self.gup_bf.ap()[e][768 + 384 * ch:768 + 384 * (ch + 1), 128 * d:128 * (d + 1)],
                      sem=z.lws[sl], transpose=True)
                z.wait("tensor", z.lws[sl], z.lws[sl].n)
                last = d == DCH - 1
                for fi in range(3):
                    z.mm(gps[fi], z.WS[:, 768 * sl + 128 * fi:768 * sl + 128 * (fi + 1)],
                         z.XOc(d, t0, t0 + 512), start=(d == 0), stop=last)
                    z.mm(ups[fi], z.WS[:, 768 * sl + 384 + 128 * fi:768 * sl + 384 + 128 * (fi + 1)],
                         z.XOc(d, t0, t0 + 512), start=(d == 0), stop=last,
                         inc=([(z.st, 1)] if fi == 2 else ()))
                ddmark[sl] = z.st.n
            self._gu_ddmark = ddmark
            gu_mm = z.st.n
            z.wait("scalar", z.st, gu_mm)
            z.wait("vector", z.st, max(gu_mm, cwb_ready))
            for fi in range(3):
                z.wait("scalar", z.sv, z.sv.n)
                z.act(z.TMP[:, 0:512], gps[fi], AF.Silu, inc=[(z.ss, 1)])
                z.wait("vector", z.ss, z.ss.n)
                z.vop("tensor_tensor", z.TMP[:, 0:512], z.TMP[:, 0:512], ups[fi], ALU.mult)
                slot = 6 * ei + 3 * ch + fi
                z.vop("tensor_tensor", z.QB[:, 512 * slot:512 * (slot + 1)],
                      z.TMP[:, 0:512], cwb, ALU.mult, inc=[(z.sv, 1)])

    def ffn_expert_down(self, eg, hf):
        z = self
        t0 = 512 * hf
        z.wait("tensor", z.sv, z.sv.n)
        z.wait("tensor", z.ss, z.ss.n)
        z.wait("vector", z.st, z.st.n)
        for dg in range(2):
            OUT = z.PS[:, 0:4096]
            smark = [z.st.n] * 24
            k = 0
            for ei in range(4):
                e = 4 * eg + ei
                for i in range(6):
                    q = "sync"
                    sem = z.ldd[k % 2]
                    stripe = z.PTW[:, 0:1024] if k % 2 == 0 else z.PTW[:, 1024:2048]
                    z.wait(q, z.st, smark[k - 2] if k >= 2 else smark[0])
                    z.dma(q, stripe, self.dp_bf.ap()[e][1024 * dg:1024 * (dg + 1), 128 * i:128 * (i + 1)],
                          sem=sem, transpose=True)
                    z.wait("tensor", sem, sem.n)
                    slot = 6 * ei + i
                    for dd in range(8):
                        z.mm(OUT[:, 512 * dd:512 * (dd + 1)], stripe[:, 128 * dd:128 * (dd + 1)],
                             z.QB[:, 512 * slot:512 * (slot + 1)], start=(k == 0),
                             stop=(k == 23),
                             inc=([(z.st, 1)] if dd == 7 else ()))
                    smark[k] = z.st.n
                    k += 1
            z.wait("vector", z.st, z.st.n)
            for dd in range(8):
                d = 8 * dg + dd
                z.vop("tensor_tensor", z.RESc(d, t0, t0 + 512), OUT[:, 512 * dd:512 * (dd + 1)],
                      z.RESc(d, t0, t0 + 512), ALU.add,
                      inc=([(z.sv, 1)] if dd == 7 else ()))
            z.wait("tensor", z.sv, z.sv.n)

    def ffn_shared(self, hf):
        z = self
        t0 = 512 * hf
        sgb = z.PS[:, 3072:3584]
        z.wait("tensor", z.sv, z.sv.n)
        z.wait("vector", z.ld, z.ld.n)
        z.dma("sync", z.R2[:, 0:512], self.sig_dram.ap()[0:1, t0:t0 + 512], sem=z.ld)
        z.wait("tensor", z.ld, z.ld.n)
        z.mm(sgb, z.ONESF[0:1, 0:128], z.R2[:, 0:512], inc=[(z.st, 1)])
        sgb_ready = z.st.n
        isbs = [(3 * k, 3) for k in range(10)] + [(30, 2)]
        ddmark = [max(z.st.n, getattr(self, "_ws_guard", 0))] * 8
        first_isb = True
        for i0, ni in isbs:
            stride = 256 * ni
            gp = [z.PS[:, 512 * f:512 * (f + 1)] for f in range(ni)]
            up = [z.PS[:, 1536 + 512 * f:1536 + 512 * (f + 1)] for f in range(ni)]
            if first_isb:
                z.wait("tensor", z.sv, z.sv.n)
                z.wait("tensor", z.ss, z.ss.n)
                first_isb = False
            for d in range(DCH):
                sl = d % 8
                z.wait("sync", z.st, ddmark[sl])
                z.wait("scalar", z.st, ddmark[sl])
                z.dma("sync", z.WS[:, 768 * sl:768 * sl + 128 * ni],
                      self.sg_bf.ap()[128 * i0:128 * (i0 + ni), 128 * d:128 * (d + 1)],
                      sem=z.lws[sl], transpose=True)
                z.dma("sync", z.WS[:, 768 * sl + 128 * ni:768 * sl + 256 * ni],
                      self.su_bf.ap()[128 * i0:128 * (i0 + ni), 128 * d:128 * (d + 1)],
                      sem=z.lws[sl], transpose=True)
                z.wait("tensor", z.lws[sl], z.lws[sl].n)
                last = d == DCH - 1
                for fi in range(ni):
                    z.mm(gp[fi], z.WS[:, 768 * sl + 128 * fi:768 * sl + 128 * (fi + 1)],
                         z.XOc(d, t0, t0 + 512), start=(d == 0), stop=last)
                    z.mm(up[fi], z.WS[:, 768 * sl + 128 * ni + 128 * fi:768 * sl + 128 * ni + 128 * (fi + 1)],
                         z.XOc(d, t0, t0 + 512), start=(d == 0), stop=last,
                         inc=([(z.st, 1)] if fi == ni - 1 else ()))
                ddmark[sl] = z.st.n
            mm_done = z.st.n
            z.wait("scalar", z.st, mm_done)
            z.wait("vector", z.st, max(mm_done, sgb_ready))
            pend_sv = z.sv.n
            for fi in range(ni):
                i = i0 + fi
                z.wait("scalar", z.sv, z.sv.n)
                z.act(z.TMP[:, 0:512], gp[fi], AF.Silu, inc=[(z.ss, 1)])
                z.wait("vector", z.ss, z.ss.n)
                z.vop("tensor_tensor", z.TMP[:, 0:512], z.TMP[:, 0:512], up[fi], ALU.mult)
                z.vop("tensor_tensor", z.QB[:, 512 * i:512 * (i + 1)],
                      z.TMP[:, 0:512], sgb, ALU.mult, inc=[(z.sv, 1)])
            z.wait("tensor", z.sv, z.sv.n)
            z.wait("tensor", z.ss, z.ss.n)
        z.wait("tensor", z.sv, z.sv.n)
        z.wait("tensor", z.ss, z.ss.n)
        z.wait("vector", z.st, z.st.n)
        for dg in range(2):
            OUT = z.PS[:, 0:4096]
            smark = [z.st.n] * 32
            for i in range(32):
                q = "sync"
                sem = z.ldd[i % 2]
                stripe = z.PTW[:, 0:1024] if i % 2 == 0 else z.PTW[:, 1024:2048]
                z.wait(q, z.st, smark[i - 2] if i >= 2 else smark[0])
                z.dma(q, stripe, self.sd_bf.ap()[1024 * dg:1024 * (dg + 1), 128 * i:128 * (i + 1)],
                      sem=sem, transpose=True)
                z.wait("tensor", sem, sem.n)
                for dd in range(8):
                    z.mm(OUT[:, 512 * dd:512 * (dd + 1)], stripe[:, 128 * dd:128 * (dd + 1)],
                         z.QB[:, 512 * i:512 * (i + 1)], start=(i == 0), stop=(i == 31),
                         inc=([(z.st, 1)] if dd == 7 else ()))
                smark[i] = z.st.n
            z.wait("vector", z.st, z.st.n)
            for dd in range(8):
                d = 8 * dg + dd
                z.vop("tensor_tensor", z.RESc(d, t0, t0 + 512), OUT[:, 512 * dd:512 * (dd + 1)],
                      z.RESc(d, t0, t0 + 512), ALU.add,
                      inc=([(z.sv, 1)] if dd == 7 else ()))
            z.wait("tensor", z.sv, z.sv.n)

    def phase_out(self):
        z = self
        self.barrier_tv()
        for d in range(DCH):
            z.dma("sync", z.yT[128 * d:128 * (d + 1), :], z.RESc(d), sem=z.so)
        z.wait("sync", z.so, z.so.n)


def _host_prepare(inputs):
    x = np.asarray(inputs["L0_attn_a_x"])
    xr = np.ascontiguousarray(x.reshape(64, 128, D))
    half = A_HD // 2
    ar = np.arange(half, dtype=np.float32) / np.float32(half)
    inv_freq = (np.float32(1.0) / (np.float32(ROPE_BASE) ** ar)).astype(np.float32)
    per_core = []
    tri = np.triu(np.ones((128, 128), dtype=np.float32)).astype(BF)
    ones = np.ones((128, 128), dtype=BF)
    zeros = np.zeros((128, 128), dtype=BF)
    for i in range(NCORE):
        xi = xr[i::8].reshape(TL, D)
        xT = np.ascontiguousarray(xi.T.astype(np.float32))
        pos = (128 * (8 * np.arange(8)[:, None] + i) + np.arange(128)[None, :]).reshape(-1).astype(np.float32)
        freqs = (inv_freq[:, None] * pos[None, :]).astype(np.float32)
        cos = np.cos(freqs).astype(np.float32)
        sin = np.sin(freqs).astype(np.float32)
        cos2 = np.ascontiguousarray(np.concatenate([cos, cos], axis=0).astype(BF))
        sin2 = np.ascontiguousarray(np.concatenate([sin, -sin], axis=0).astype(BF))
        m = np.stack([ones if r < i else (tri if r == i else zeros) for r in range(NCORE)])
        per_core.append({"xT": xT, "cos2": cos2, "sin2": sin2, "masks": np.ascontiguousarray(m)})
    return per_core


_BUILD_CACHE = {}


def _get_nc(stages=None, taps=None):
    key = (stages if stages is not None else BUILD_STAGES, tuple(taps or ()))
    if key not in _BUILD_CACHE:
        mk = MK()
        nc = mk.build(stages=key[0], taps=key[1])
        _BUILD_CACHE[key] = (mk, nc)
    return _BUILD_CACHE[key]


def kernel(**inputs):
    per_core = _host_prepare(inputs)
    mk, nc = _get_nc()
    f32 = lambda k: np.ascontiguousarray(np.asarray(inputs[k], dtype=np.float32))
    lrows = np.concatenate([
        np.asarray(inputs["L0_attn_b_lambda_q1"], dtype=np.float32).reshape(-1),
        np.asarray(inputs["L0_attn_b_lambda_k1"], dtype=np.float32).reshape(-1),
        np.asarray(inputs["L0_attn_b_lambda_q2"], dtype=np.float32).reshape(-1),
        np.asarray(inputs["L0_attn_b_lambda_k2"], dtype=np.float32).reshape(-1),
    ]).reshape(1, 256)
    shared = {
        "a_wq": f32("L0_attn_a_wq"), "a_wk": f32("L0_attn_a_wk"),
        "a_wv": f32("L0_attn_a_wv"), "a_wo": f32("L0_attn_a_wo"),
        "b_wqkv": f32("L0_attn_b_Wqkv"), "b_wproj": f32("L0_attn_b_Wproj"),
        "lrows": lrows,
        "gate_w": f32("L0_ffn_gate_w"), "gup": f32("L0_ffn_gate_up_proj"),
        "dpw": f32("L0_ffn_down_proj"), "sgw": f32("L0_ffn_shared_gate_w"),
        "suw": f32("L0_ffn_shared_up_w"), "sdw": f32("L0_ffn_shared_down_w"),
        "segw": f32("L0_ffn_shared_expert_gate_w"),
    }
    in_maps = []
    for i in range(NCORE):
        m = dict(shared)
        m.update(per_core[i])
        in_maps.append(m)
    res = run_bass_kernel_spmd(nc, in_maps, core_ids=list(range(NCORE)))
    y = np.empty((1, T, D), dtype=np.float32)
    yr = y.reshape(64, 128, D)
    for i in range(NCORE):
        yr[i::8] = res.results[i]["yT"].T.reshape(8, 128, D)
    return y
